# version 05
# baseline (speedup 1.0000x reference)
import os
import numpy as np
import ml_dtypes
import concourse.bass as bass
import concourse.mybir as mybir
from concourse.bass_utils import run_bass_kernel_spmd
from contextlib import ExitStack

F32 = mybir.dt.float32
BF16 = mybir.dt.bfloat16
AF = mybir.ActivationFunctionType
ALU = mybir.AluOpType
AX = mybir.AxisListType

T = 4096
D = 1024
I2 = 2048
NKT = 8
NTC = 8
TC = 512
NE = 8
NHP = 8
EPS = 1e-5

ENGS = ("sync", "gpsimd", "vector", "scalar", "tensor")

AR_N = 39808

A_WG = 0
A_WU = 8192
A_WD = 16384
A_XS = 24576
A_ACT = 32768

A_F32SCR = 0
A_XNCH = 8192
A_XSE = 12288
A_SQ = 16384
A_ROWS = 20480
A_LOGSB = 27648
A_LROW = 0
A_SEROW = 28672
A_XSE1 = 32768

A_QA = 0
A_QB = 4096
A_KR = 8192
A_VT = 12288
VTS = 132
A_ROPC = 16512
A_ROPS = 20608
A_DMSK = 24704
A_XN2 = 26752
A_QRAW = 30848
A_KRAW = 31360
A_VTMP = 31872
A_SCA = 32384
A_SCB = 32896
A_PT = 33408
A_RROW = 35456
A_STAT = 35968
A_FRBC = 31872
A_FOUT = 30848

A_WO = 0
A_AST = 8192


class Plan:
    def __init__(self):
        self.ops = {e: [] for e in ENGS}
        self.n = {e: 0 for e in ENGS}

    def emit(self, e, fn=None, waits=(), inc=0, bump=0):
        ws = tuple((we, wv) for (we, wv) in waits if wv > 0)
        self.ops[e].append((ws, fn, inc))
        self.n[e] += inc + bump
        return self.n[e]

    def barrier(self):
        snap = dict(self.n)
        for e in ENGS:
            self.emit(e, None, waits=[(o, snap[o]) for o in ENGS if o != e])
        return snap


def _bf(x):
    return np.ascontiguousarray(np.asarray(x).astype(ml_dtypes.bfloat16))


_HC = None


def host_constants():
    global _HC
    if _HC is not None:
        return _HC
    theta = 10000.0
    pos = np.arange(T, dtype=np.float64)
    kk = np.arange(16, dtype=np.float64)
    k32 = np.arange(32, dtype=np.float64)
    angle = pos[None, :] / theta ** (2.0 * k32[:, None] / 64.0)
    sin, cos = np.sin(angle), np.cos(angle)
    ctab = np.zeros((128, T))
    stab = np.zeros((128, T))
    for f in range(64):
        ctab[f] = cos[f // 2]
        ctab[64 + f] = cos[f // 2]
        s = -sin[f // 2] if f % 2 == 0 else sin[f // 2]
        stab[f] = s
        stab[64 + f] = s
    swap = np.zeros((128, 128))
    for r in range(0, 128, 2):
        swap[r + 1, r] = 1.0
        swap[r, r + 1] = 1.0
    dm = np.zeros((128, 4, 512))
    for j in range(4):
        dm[:, j, :] = (128 * j + np.arange(128)[:, None] <= np.arange(512)[None, :])
    _HC = dict(
        ropec=_bf(ctab), ropes=_bf(stab), swapm=_bf(swap), identb=_bf(np.eye(128)),
        dmask=_bf(dm.reshape(128, 2048)), ones2d=_bf(np.ones((128, 128))),
        onescol_f=np.ones((128, 1), dtype=np.float32),
        onescol_b=_bf(np.ones((128, 1))),
        onesrow_f=np.ones((1, 128), dtype=np.float32),
        onesrow_b=_bf(np.ones((1, 128))),
        onesa=_bf(np.concatenate([np.ones((64, 1)), np.zeros((64, 1))])),
        onesb_=_bf(np.concatenate([np.zeros((64, 1)), np.ones((64, 1))])),
    )
    return _HC


def build_program(stage="full", debug=False):
    nc = bass.Bass("TRN2", target_bir_lowering=False)
    din = lambda nm, shp, dt: nc.dram_tensor(nm, shp, dt, kind="ExternalInput").ap()
    xT = din("xT", [D, T], F32)
    wqT = din("wqT", [D, D], F32)
    wkT = din("wkT", [D, D], F32)
    wvT = din("wvT", [D, D], F32)
    woT = din("woT", [D, D], F32)
    gw = din("gw", [D, NE], F32)
    shg = din("shg", [D, I2], F32)
    shu = din("shu", [D, I2], F32)
    shd = din("shd", [I2, D], F32)
    rtg = din("rtg", [NE, D, I2], F32)
    rtu = din("rtu", [NE, D, I2], F32)
    rtd = din("rtd", [NE, I2, D], F32)
    hc = host_constants()
    cst = {k: din(k, list(v.shape), BF16 if v.dtype == ml_dtypes.bfloat16 else F32)
           for k, v in hc.items()}
    outT = nc.dram_tensor("outT", [D, T], F32, kind="ExternalOutput").ap()
    dbg_kind = "ExternalOutput" if debug else "Internal"
    xn_h = nc.dram_tensor("xn_h", [128, NKT, T], BF16, kind=dbg_kind).ap()
    xs_h = nc.dram_tensor("xs_h", [NE, 128, NKT, T], BF16, kind=dbg_kind).ap()
    attn_h = nc.dram_tensor("attn_h", [NHP, 128, T], BF16, kind=dbg_kind).ap()

    with ExitStack() as ctx:
        E = ctx.enter_context
        R = E(nc.sbuf_tensor("R", [128, NKT * T], F32))
        AR = E(nc.sbuf_tensor("AR", [128, AR_N], BF16))
        ARF = AR.bitcast(F32)
        sb = {}
        for k, v in hc.items():
            if k in ("ropec", "ropes", "dmask"):
                continue
            dt = BF16 if v.dtype == ml_dtypes.bfloat16 else F32
            sb[k] = E(nc.sbuf_tensor("c_" + k, list(v.shape), dt))
        gwsb = E(nc.sbuf_tensor("gwsb", [128, NKT, NE], F32))
        biasA = E(nc.sbuf_tensor("biasA", [128, 1], F32))
        biasB = E(nc.sbuf_tensor("biasB", [128, 1], F32))
        runm = E(nc.sbuf_tensor("runm", [1, 8], F32))
        tiny = E(nc.sbuf_tensor("tiny", [1, 8], F32))
        tinyb = E(nc.sbuf_tensor("tinyb", [1, 8], BF16))
        PSW = [E(nc.psum_tensor(f"PSW{i}", [128, 1024], F32)) for i in range(2)]
        PS47 = [E(nc.psum_tensor(f"PS{i}", [128, 512], F32)) for i in range(4, 8)]
        PS = [PSW[0][:, 0:512], PSW[0][:, 512:1024],
              PSW[1][:, 0:512], PSW[1][:, 512:1024]] + [p[:] for p in PS47]
        PSB = [None, None, None, None] + [p.bitcast(BF16)[:] for p in PS47]
        PSBW0 = PSW[0].bitcast(BF16)
        sem = {e: E(nc.semaphore("s_" + e)) for e in ENGS}
        block = E(nc.Block())

        pl = Plan()

        def Rsl(kt, lo, ln):
            return R[:, kt * T + lo: kt * T + lo + ln]

        def ar(lo, ln, part=(0, 128)):
            return AR[part[0]:part[1], lo:lo + ln]

        def arf(lo_bf, ln, part=(0, 128)):
            assert lo_bf % 2 == 0
            return ARF[part[0]:part[1], lo_bf // 2: lo_bf // 2 + ln]

        def init_phase():
            pl.emit("sync", lambda s: s.dma_start(
                out=R[:].rearrange("p (kt t) -> p kt t", t=T),
                in_=xT.rearrange("(kt p) t -> p kt t", p=128)), inc=16)
            for k in hc:
                if k in ("ropec", "ropes", "dmask"):
                    continue
                pl.emit("sync", (lambda k: lambda s: s.dma_start(out=sb[k][:], in_=cst[k]))(k), inc=16)
            pl.emit("sync", lambda s: s.dma_start(
                out=gwsb[:], in_=gw.rearrange("(kt p) e -> p kt e", p=128)), inc=16)
            pl.barrier()

        def ln_phase(routing):
            r_mean = arf(A_ROWS, 512, (0, 1))
            r_B = arf(A_ROWS + 1024, 512, (0, 1))
            r_var = arf(A_ROWS + 2048, 512, (0, 1))
            r_lnv = r_B
            r_rstd = arf(A_ROWS + 3072, 512, (0, 1))
            r_top = arf(A_ROWS + 4096, 512, (0, 1))
            r_sco = arf(A_ROWS + 5120, 512, (0, 1))
            r_tmp = arf(A_ROWS + 6144, 512, (0, 1))
            lrow = [arf(A_LROW + e * 1024, 512, (0, 1)) for e in range(NE)]
            se_rows = [ar(A_SEROW + e * TC, TC, (0, 1)) for e in range(NE)]
            xse_slot = [A_XSE, A_XSE1]
            c_ssq_prev = 0
            c_xndma_prev = 0
            c_rd_prev = 0
            c_xsd_hist = {}
            c_row1_hist = {}
            c_lg_prev = 0
            c_bc_prev = 0
            c_bcse_prev = 0
            c_xs_prev = 0
            nxs = 0
            pending_xs = None
            for tc in range(NTC):
                if pending_xs is not None:
                    pending_xs()
                    pending_xs = None
                lo = tc * TC
                sB = PS[0] if tc % 2 == 0 else PS[6]
                qB = PS[1] if tc % 2 == 0 else PS[7]
                def fsum(t, lo=lo, sB=sB):
                    for kt in range(NKT):
                        mm = t.matmul(sB[0:1, :], sb["onescol_f"][:], Rsl(kt, lo, TC),
                                      start=(kt == 0), stop=(kt == NKT - 1))
                    return mm
                c_sum = pl.emit("tensor", fsum,
                                waits=[("vector", c_row1_hist.get(tc - 2, 0))], inc=1)
                def fsq(s, lo=lo):
                    return s.activation(
                        ar(A_SQ, 4096).rearrange("p (kt t) -> p kt t", t=TC),
                        R[:].rearrange("p (kt t) -> p kt t", t=T)[:, :, lo:lo + TC],
                        AF.Square, bias=0.0, scale=1.0)
                c_sq = pl.emit("scalar", fsq, waits=[("tensor", c_ssq_prev)], inc=1)
                def fssq(t, qB=qB):
                    for kt in range(NKT):
                        mm = t.matmul(qB[0:1, :], sb["onescol_b"][:], ar(A_SQ + kt * TC, TC),
                                      start=(kt == 0), stop=(kt == NKT - 1))
                    return mm
                c_ssq = pl.emit("tensor", fssq, waits=[("scalar", c_sq)], inc=1)
                c_ssq_prev = c_ssq
                def frow1(v, sB=sB, qB=qB):
                    v.tensor_scalar(r_mean, sB[0:1, :], 1.0 / D, None, ALU.mult)
                    v.tensor_scalar(r_var, qB[0:1, :], 1.0 / D, EPS, ALU.mult, ALU.add)
                    v.tensor_tensor(r_tmp, r_mean, r_mean, ALU.mult)
                    return v.tensor_sub(r_var, r_var, r_tmp)
                c_row1 = pl.emit("vector", frow1, waits=[("tensor", c_ssq)], inc=1)
                c_row1_hist[tc] = c_row1
                def frow2(s):
                    s.activation(r_lnv, r_var, AF.Ln, bias=0.0, scale=1.0)
                    return s.activation(r_rstd, r_lnv, AF.Exp, bias=0.0, scale=-0.5)
                c_row2 = pl.emit("scalar", frow2, waits=[("vector", c_row1),
                                                         ("tensor", c_bc_prev)], inc=1)
                def frow3(v):
                    return v.tensor_tensor(r_B, r_mean, r_rstd, ALU.mult)
                c_row3 = pl.emit("vector", frow3, waits=[("scalar", c_row2)], inc=1)
                def fbc(t):
                    t.matmul(PS[2][:, :], sb["onesrow_f"][:], r_rstd, start=True, stop=True)
                    return t.matmul(PS[3][:, :], sb["onesrow_f"][:], r_B, start=True, stop=True)
                c_bc = pl.emit("tensor", fbc, waits=[("vector", c_row3)], inc=1)
                c_bc_prev = c_bc
                def fxn(v, lo=lo):
                    for kt in range(NKT):
                        v.tensor_tensor(arf(A_F32SCR + kt * 1024, TC), Rsl(kt, lo, TC),
                                        PS[2][:, :], ALU.mult)
                    for kt in range(NKT):
                        last = v.tensor_sub(arf(A_F32SCR + kt * 1024, TC),
                                            arf(A_F32SCR + kt * 1024, TC), PS[3][:, :])
                    return last
                c_xnf = pl.emit("vector", fxn, waits=[("tensor", c_bc),
                                                      ("tensor", c_lg_prev)], inc=1)
                def fxnb(v):
                    return v.tensor_copy(ar(A_XNCH, 4096), arf(A_F32SCR, 4096))
                c_xnb = pl.emit("vector", fxnb, waits=[("sync", c_xndma_prev)], inc=1)
                def fxnd(s, tc=tc):
                    return s.dma_start(out=xn_h[:, :, tc * TC:(tc + 1) * TC],
                                       in_=ar(A_XNCH, 4096).rearrange("p (kt t) -> p kt t", t=TC))
                c_xndma = pl.emit("sync", fxnd, waits=[("vector", c_xnb)], inc=16)
                c_xndma_prev = c_xndma

                if routing:
                    def flg(t):
                        for kt in range(NKT):
                            mm = t.matmul(PS[4][0:8, :], gwsb[:, kt, :], arf(A_F32SCR + kt * 1024, TC),
                                          start=(kt == 0), stop=(kt == NKT - 1))
                        return mm
                    c_lg = pl.emit("tensor", flg, waits=[("vector", c_xnf)], inc=1)
                    c_lg_prev = c_lg
                    def flev(v):
                        return v.tensor_copy(arf(A_LOGSB, 512, (0, 8)), PS[4][0:8, :])
                    c_lev = pl.emit("vector", flev, waits=[("tensor", c_lg),
                                                           ("sync", c_rd_prev)], inc=1)
                    def frd(s):
                        for e in range(NE - 1):
                            s.dma_start(out=lrow[e], in_=ARF[e:e + 1, A_LOGSB // 2: A_LOGSB // 2 + 512]).then_inc(sem["sync"], 16)
                        return s.dma_start(out=lrow[NE - 1], in_=ARF[NE - 1:NE, A_LOGSB // 2: A_LOGSB // 2 + 512])
                    c_rd = pl.emit("sync", frd, waits=[("vector", c_lev)], inc=16, bump=16 * (NE - 1))
                    c_rd_prev = c_rd
                    def ftop(v):
                        v.tensor_tensor(r_top, lrow[0], lrow[1], ALU.max)
                        for e in range(2, NE):
                            last = v.tensor_tensor(r_top, r_top, lrow[e], ALU.max)
                        return last
                    c_top = pl.emit("vector", ftop, waits=[("sync", c_rd)], inc=1)
                    def fsco(s):
                        return s.activation(r_sco, r_top, AF.Sigmoid, bias=0.0, scale=1.0)
                    c_sco = pl.emit("scalar", fsco, waits=[("vector", c_top)], inc=1)
                    def fseb(v):
                        for e in range(NE):
                            v.tensor_tensor(r_tmp, lrow[e], r_top, ALU.is_ge)
                            last = v.tensor_tensor(se_rows[e], r_tmp, r_sco, ALU.mult)
                        return last
                    c_seb = pl.emit("vector", fseb, waits=[("scalar", c_sco),
                                                           ("tensor", c_bcse_prev)], inc=1)

                    def mk_xs(tc=tc, c_seb=c_seb):
                        def run():
                            nonlocal c_bcse_prev, c_xs_prev, nxs
                            for e in range(NE):
                                def fbcse(t, e=e):
                                    return t.matmul(PS[5][:, :], sb["onesrow_b"][:], se_rows[e],
                                                    start=True, stop=True)
                                c_bcse = pl.emit("tensor", fbcse, waits=[("vector", c_seb),
                                                                         ("vector", c_xs_prev)], inc=1)
                                c_bcse_prev = c_bcse
                                slot = xse_slot[nxs % 2]
                                def fxs(v, slot=slot):
                                    for kt in range(NKT):
                                        last = v.tensor_tensor(ar(slot + kt * TC, TC),
                                                               ar(A_XNCH + kt * TC, TC),
                                                               PS[5][:, :], ALU.mult)
                                    return last
                                c_xs = pl.emit("vector", fxs, waits=[("tensor", c_bcse),
                                                                     ("sync", c_xsd_hist.get(nxs - 2, 0))], inc=1)
                                c_xs_prev = c_xs
                                def fxsd(s, e=e, tc=tc, slot=slot):
                                    return s.dma_start(out=xs_h[e][:, :, tc * TC:(tc + 1) * TC],
                                                       in_=ar(slot, 4096).rearrange("p (kt t) -> p kt t", t=TC))
                                c_xsd_hist[nxs] = pl.emit("sync", fxsd, waits=[("vector", c_xs)], inc=16)
                                nxs += 1
                        return run
                    pending_xs = mk_xs()
            if pending_xs is not None:
                pending_xs()
                pending_xs = None
            pl.barrier()

        def expert_phase():
            units = [(e, h) for e in range(NE + 1) for h in range(2)]
            gtc = 0
            pe_after_gu = pl.n["tensor"]
            pe_after_dn = pl.n["tensor"]
            radd_last = pl.n["vector"]
            pending_down = None
            gu_hist = {}
            for ui, (e, ih) in enumerate(units):
                if pending_down is not None:
                    pe_after_dn, radd_last = pending_down()
                    pending_down = None
                if e < NE:
                    srcg, srcu, srcd = rtg[e], rtu[e], rtd[e]
                else:
                    srcg, srcu, srcd = shg, shu, shd
                def fwgu(g, srcg=srcg, srcu=srcu, ih=ih):
                    g.dma_start(out=ar(A_WG, 8192).rearrange("p (kt i) -> p kt i", i=1024),
                                in_=srcg.rearrange("(kt p) i -> p kt i", p=128)[:, :, ih * 1024:(ih + 1) * 1024]).then_inc(sem["gpsimd"], 16)
                    return g.dma_start(out=ar(A_WU, 8192).rearrange("p (kt i) -> p kt i", i=1024),
                                       in_=srcu.rearrange("(kt p) i -> p kt i", p=128)[:, :, ih * 1024:(ih + 1) * 1024])
                c_wgu = pl.emit("gpsimd", fwgu, waits=[("tensor", pe_after_gu)], inc=16, bump=16)
                def fwd(g, srcd=srcd, ih=ih):
                    return g.dma_start(out=ar(A_WD, 8192).rearrange("p (kt h) -> p kt h", h=1024),
                                       in_=srcd.rearrange("(ikt p) h -> p ikt h", p=128)[:, 8 * ih:8 * ih + 8, :])
                c_wd = pl.emit("gpsimd", fwd, waits=[("tensor", pe_after_dn)], inc=16)

                silu_done = {}
                ustt_done = {}
                for tc in range(NTC):
                    if pending_down is not None:
                        pe_after_dn, radd_last = pending_down()
                        pending_down = None
                    par = gtc % 2
                    src_h = xs_h[e] if e < NE else xn_h
                    def fxs(s, src_h=src_h, tc=tc, par=par):
                        return s.dma_start(out=ar(A_XS + par * 4096, 4096)
                                           .rearrange("p (kt t) -> p kt t", t=TC),
                                           in_=src_h[:, :, tc * TC:(tc + 1) * TC])
                    xs_w = [("tensor", gu_hist.get(gtc - 2, 0))]
                    c_xsl = pl.emit("sync", fxs, waits=xs_w, inc=16)
                    for it in range(NKT):
                        gb = PS[it % 2]
                        ub = PS[2 + it % 2]
                        def fg(t, it=it, par=par, gb=gb):
                            for kt in range(NKT):
                                mm = t.matmul(gb[:, :], ar(A_WG + kt * 1024 + it * 128, 128),
                                              ar(A_XS + par * 4096 + kt * TC, TC),
                                              start=(kt == 0), stop=(kt == NKT - 1))
                            return mm
                        w_g = [("gpsimd", c_wgu), ("sync", c_xsl)]
                        if (tc, it - 2) in silu_done:
                            w_g.append(("scalar", silu_done[(tc, it - 2)]))
                        elif (tc - 1, 6 + it) in silu_done:
                            w_g.append(("scalar", silu_done[(tc - 1, 6 + it)]))
                        c_g = pl.emit("tensor", fg, waits=w_g, inc=1)
                        def fu(t, it=it, par=par, ub=ub):
                            for kt in range(NKT):
                                mm = t.matmul(ub[:, :], ar(A_WU + kt * 1024 + it * 128, 128),
                                              ar(A_XS + par * 4096 + kt * TC, TC),
                                              start=(kt == 0), stop=(kt == NKT - 1))
                            return mm
                        w_u = []
                        if (tc, it - 2) in ustt_done:
                            w_u.append(("vector", ustt_done[(tc, it - 2)]))
                        elif (tc - 1, 6 + it) in ustt_done:
                            w_u.append(("vector", ustt_done[(tc - 1, 6 + it)]))
                        c_u = pl.emit("tensor", fu, waits=w_u, inc=1)
                        gu_hist[gtc] = c_u
                        def fsl(s, it=it, gb=gb):
                            return s.activation(ar(A_ACT + it * TC, TC), gb[:, :], AF.Silu,
                                                bias=0.0, scale=1.0)
                        silu_done[(tc, it)] = pl.emit("scalar", fsl, waits=[("tensor", c_g)], inc=1)
                        def fus(v, it=it, ub=ub):
                            return v.scalar_tensor_tensor(ar(A_ACT + it * TC, TC), ub[:, :], 0.5,
                                                          ar(A_ACT + it * TC, TC), ALU.mult, ALU.mult)
                        ustt_done[(tc, it)] = pl.emit("vector", fus,
                                                      waits=[("tensor", c_u),
                                                             ("scalar", silu_done[(tc, it)])], inc=1)
                    pe_after_gu = gu_hist[gtc]
                    def mk_down(tc=tc, ustt_val=ustt_done[(tc, NKT - 1)], c_wd=c_wd):
                        def run():
                            last_pe = 0
                            last_radd = radd_last
                            for tq in range(2):
                                y_done = None
                                for ht in range(NKT):
                                    yb = PS[4 + ht % 4]
                                    ycols = (ht // 4) * 256
                                    def fd(t, ht=ht, tq=tq, tc=tc, yb=yb, ycols=ycols):
                                        for ikt in range(NKT):
                                            mm = t.matmul(yb[:, ycols:ycols + 256],
                                                          ar(A_WD + ikt * 1024 + ht * 128, 128),
                                                          ar(A_ACT + ikt * TC + tq * 256, 256),
                                                          start=(ikt == 0), stop=(ikt == NKT - 1))
                                        return mm
                                    wd_w = [("gpsimd", c_wd), ("vector", ustt_val)]
                                    if ht == 0:
                                        wd_w.append(("vector", last_radd))
                                    y_done = pl.emit("tensor", fd, waits=wd_w, inc=1)
                                def fra(v, tq=tq, tc=tc):
                                    for ht in range(NKT):
                                        last = v.tensor_add(Rsl(ht, tc * TC + tq * 256, 256),
                                                            Rsl(ht, tc * TC + tq * 256, 256),
                                                            PS[4 + ht % 4][:, (ht // 4) * 256:(ht // 4) * 256 + 256])
                                    return last
                                last_radd = pl.emit("vector", fra, waits=[("tensor", y_done)], inc=1)
                                last_pe = y_done
                            return last_pe, last_radd
                        return run
                    pending_down = mk_down()
                    gtc += 1
            if pending_down is not None:
                pe_after_dn, radd_last = pending_down()
            pl.barrier()

        def attn_phase():
            def fld(s):
                s.dma_start(out=ar(A_ROPC, T), in_=cst["ropec"]).then_inc(sem["sync"], 16)
                s.dma_start(out=ar(A_ROPS, T), in_=cst["ropes"]).then_inc(sem["sync"], 16)
                return s.dma_start(out=ar(A_DMSK, 2048), in_=cst["dmask"])
            pl.emit("sync", fld, inc=16, bump=32)
            pl.barrier()
            for hp in range(NHP):
                def fwst(g, hp=hp):
                    g.dma_start(out=ar(A_STAT, 1024).rearrange("p (kt c) -> p kt c", c=128),
                                in_=wqT.rearrange("(kt p) c -> p kt c", p=128)[:, :, hp * 128:(hp + 1) * 128]).then_inc(sem["gpsimd"], 16)
                    g.dma_start(out=ar(A_STAT + 1024, 1024).rearrange("p (kt c) -> p kt c", c=128),
                                in_=wkT.rearrange("(kt p) c -> p kt c", p=128)[:, :, hp * 128:(hp + 1) * 128]).then_inc(sem["gpsimd"], 16)
                    return g.dma_start(out=ar(A_STAT + 2048, 1024).rearrange("p (kt c) -> p kt c", c=128),
                                       in_=wvT.rearrange("(kt p) c -> p kt c", p=128)[:, :, hp * 128:(hp + 1) * 128])
                c_wst = pl.emit("gpsimd", fwst, waits=[("tensor", pl.n["tensor"])], inc=16, bump=32)
                def fz(v):
                    v.memset(ar(A_QA, T, (64, 128)), 0.0)
                    v.memset(ar(A_QB, T, (0, 64)), 0.0)
                    return v.memset(runm[:], -3.0e38)
                c_z = pl.emit("vector", fz, waits=[("tensor", pl.n["tensor"]),
                                                   ("scalar", pl.n["scalar"]),
                                                   ("sync", pl.n["sync"])], inc=1)
                ev = {"evQ": 0, "evK": 0, "vch": 0, "ropeK": 0}
                vtr = {}
                for c in range(NTC):
                    lo = c * TC
                    def fxnc(s, c=c):
                        return s.dma_start(out=ar(A_XN2, 4096).rearrange("p (kt t) -> p kt t", t=TC),
                                           in_=xn_h[:, :, c * TC:(c + 1) * TC])
                    c_xn = pl.emit("sync", fxnc, waits=[("tensor", ev["vch"])], inc=16)
                    def fq(t):
                        for kt in range(NKT):
                            mm = t.matmul(PS[6][:, :], ar(A_STAT + kt * 128, 128),
                                          ar(A_XN2 + kt * TC, TC), start=(kt == 0), stop=(kt == NKT - 1))
                        return mm
                    c_q = pl.emit("tensor", fq, waits=[("sync", c_xn), ("gpsimd", c_wst),
                                                       ("scalar", ev["evQ"]), ("vector", c_z)], inc=1)
                    def feq(s):
                        return s.mul(ar(A_QRAW, TC), PS[6][:, :], 0.125)
                    c_eq = pl.emit("scalar", feq, waits=[("tensor", c_q)], inc=1)
                    def fk(t):
                        for kt in range(NKT):
                            mm = t.matmul(PS[7][:, :], ar(A_STAT + 1024 + kt * 128, 128),
                                          ar(A_XN2 + kt * TC, TC), start=(kt == 0), stop=(kt == NKT - 1))
                        return mm
                    c_k = pl.emit("tensor", fk, waits=[("scalar", ev["evK"])], inc=1)
                    def fek(s):
                        return s.copy(ar(A_KRAW, TC), PS[7][:, :])
                    c_ek = pl.emit("scalar", fek, waits=[("tensor", c_k)], inc=1)
                    ev["evK"] = c_ek
                    def fv(t):
                        for kt in range(NKT):
                            mm = t.matmul(PS[6][:, :], ar(A_STAT + 2048 + kt * 128, 128),
                                          ar(A_XN2 + kt * TC, TC), start=(kt == 0), stop=(kt == NKT - 1))
                        return mm
                    c_v = pl.emit("tensor", fv, waits=[("scalar", c_eq)], inc=1)
                    ev["vch"] = c_v
                    def fev(s):
                        return s.copy(ar(A_VTMP, TC), PS[6][:, :])
                    c_ev = pl.emit("scalar", fev, waits=[("tensor", c_v)], inc=1)
                    ev["evQ"] = c_ev
                    NOROPE = os.environ.get("KNOROPE", "") == "1"
                    if NOROPE:
                        def fcpq(v, lo=lo):
                            v.tensor_copy(ar(A_QA + lo, TC, (0, 64)), ar(A_QRAW, TC, (0, 64)))
                            v.tensor_copy(ar(A_QB + lo, TC, (64, 128)), ar(A_QRAW, TC, (64, 128)))
                            return v.tensor_copy(ar(A_KR + lo, TC), ar(A_KRAW, TC))
                        c_rk = pl.emit("vector", fcpq, waits=[("scalar", c_ek),
                                                              ("vector", ev["ropeK"])], inc=1)
                        ev["ropeK"] = c_rk
                    def fsw(t):
                        return t.matmul(PS[5][:, :], sb["swapm"][:], ar(A_QRAW, TC),
                                        start=True, stop=True)
                    c_sw = (pl.n["tensor"] if NOROPE else
                            pl.emit("tensor", fsw, waits=[("scalar", c_eq),
                                                          ("vector", ev["ropeK"])], inc=1))
                    def frq(v, lo=lo):
                        v.tensor_tensor(ar(A_SCA, TC), ar(A_QRAW, TC), ar(A_ROPC + lo, TC), ALU.mult)
                        v.tensor_tensor(ar(A_SCB, TC), PS[5][:, :], ar(A_ROPS + lo, TC), ALU.mult)
                        v.tensor_add(ar(A_QA + lo, TC, (0, 64)), ar(A_SCA, TC, (0, 64)),
                                     ar(A_SCB, TC, (0, 64)))
                        return v.tensor_add(ar(A_QB + lo, TC, (64, 128)), ar(A_SCA, TC, (64, 128)),
                                            ar(A_SCB, TC, (64, 128)))
                    c_rq = 0 if NOROPE else pl.emit("vector", frq, waits=[("tensor", c_sw)], inc=1)
                    def fswk(t):
                        return t.matmul(PS[5][:, :], sb["swapm"][:], ar(A_KRAW, TC),
                                        start=True, stop=True)
                    c_swk = (pl.n["tensor"] if NOROPE else
                             pl.emit("tensor", fswk, waits=[("scalar", c_ek), ("vector", c_rq)], inc=1))
                    def frk(v, lo=lo):
                        v.tensor_tensor(ar(A_SCA, TC), ar(A_KRAW, TC), ar(A_ROPC + lo, TC), ALU.mult)
                        v.tensor_tensor(ar(A_SCB, TC), PS[5][:, :], ar(A_ROPS + lo, TC), ALU.mult)
                        return v.tensor_add(ar(A_KR + lo, TC), ar(A_SCA, TC), ar(A_SCB, TC))
                    if not NOROPE:
                        c_rk = pl.emit("vector", frk, waits=[("tensor", c_swk)], inc=1)
                        ev["ropeK"] = c_rk
                    for j in range(4):
                        KNV = os.environ.get("KNOVT", "")
                        if KNV == "1":
                            break
                        tj = 4 * c + j
                        def ftr(t, j=j):
                            tout = PSB[4][:, 0:128] if j % 2 == 0 else PSBW0[:, 0:128]
                            t.transpose(tout, ar(A_VTMP + j * 128, 128), sb["identb"][:])
                            return t.matmul(PS[0][0:1, 0:8], sb["onesrow_b"][0:1, 0:1],
                                            sb["onesrow_b"][0:1, 0:8], start=True, stop=True)
                        w_tr = [("scalar", c_ev)]
                        if (j - 2) in vtr:
                            w_tr.append(("vector", vtr[j - 2]))
                        c_tr = pl.emit("tensor", ftr, waits=w_tr, inc=1)
                        if KNV == "2":
                            def fcv(v, j=j, tj=tj):
                                return v.tensor_copy(ar(A_VT + tj * 130, 64), ar(A_VTMP, 64))
                        elif KNV == "3":
                            def fcv(v, j=j, tj=tj):
                                tin = PSB[4] if j % 2 == 0 else PSBW0
                                v.tensor_copy(ar(A_VT + tj * VTS, 64), tin[:, 0:64])
                                return v.tensor_copy(ar(A_VT + tj * VTS + 66, 64), tin[:, 64:128])
                        else:
                            def fcv(v, j=j, tj=tj):
                                tin = PSB[4] if j % 2 == 0 else PSBW0
                                v.tensor_copy(ar(A_VT + tj * VTS, 64), tin[:, 0:64])
                                v.memset(ar(A_VT + tj * VTS + 64, 2), 1.0)
                                v.tensor_copy(ar(A_VT + tj * VTS + 66, 64), tin[:, 64:128])
                                return v.memset(ar(A_VT + tj * VTS + 130, 2), 1.0)
                        vtr[j] = pl.emit("vector", fcv, waits=[("tensor", c_tr)], inc=1)
                    if 3 in vtr:
                        pl.emit("tensor", None, waits=[("vector", vtr[3])])
                if os.environ.get("KATTNCUT", "") == "proj":
                    pl.barrier()
                    continue
                c_rm = 0
                for c in range(NTC):
                    lo = c * TC
                    def fsq2(s, lo=lo):
                        s.activation(ar(A_QRAW, TC), ar(A_QA + lo, TC), AF.Square, bias=0.0, scale=1.0)
                        s.activation(ar(A_KRAW, TC), ar(A_QB + lo, TC), AF.Square, bias=0.0, scale=1.0)
                        return s.activation(ar(A_VTMP, TC), ar(A_KR + lo, TC), AF.Square, bias=0.0, scale=1.0)
                    c_s2 = pl.emit("scalar", fsq2, waits=[("vector", pl.n["vector"]),
                                                          ("tensor", pl.n["tensor"])], inc=1)
                    def fnm(t):
                        t.matmul(PS[3][0:1, :], sb["onescol_b"][:], ar(A_QRAW, TC), start=True, stop=True)
                        t.matmul(PS[6][0:1, :], sb["onescol_b"][:], ar(A_KRAW, TC), start=True, stop=True)
                        t.matmul(PS[7][0:1, :], sb["onesa"][:], ar(A_VTMP, TC), start=True, stop=True)
                        return t.matmul(PS[5][0:1, :], sb["onesb_"][:], ar(A_VTMP, TC), start=True, stop=True)
                    c_nm = pl.emit("tensor", fnm, waits=[("scalar", c_s2), ("vector", c_rm)], inc=1)
                    def frm(v):
                        v.reduce_max(tiny[:, 0:1], PS[3][0:1, :], axis=AX.X)
                        v.reduce_max(tiny[:, 1:2], PS[6][0:1, :], axis=AX.X)
                        v.reduce_max(tiny[:, 2:3], PS[7][0:1, :], axis=AX.X)
                        v.reduce_max(tiny[:, 3:4], PS[5][0:1, :], axis=AX.X)
                        return v.tensor_tensor(runm[:, 0:4], runm[:, 0:4], tiny[:, 0:4], ALU.max)
                    c_rm = pl.emit("vector", frm, waits=[("tensor", c_nm)], inc=1)
                def fmh(v):
                    v.tensor_tensor(tiny[:, 0:1], runm[:, 0:1], runm[:, 2:3], ALU.mult)
                    v.tensor_tensor(tiny[:, 1:2], runm[:, 1:2], runm[:, 3:4], ALU.mult)
                    return v.tensor_tensor(tiny[:, 0:1], tiny[:, 0:1], tiny[:, 1:2], ALU.max)
                c_mh = pl.emit("vector", fmh, waits=[("vector", c_rm)], inc=1)
                c_mh2 = pl.emit("scalar", lambda s: s.activation(tiny[:, 2:3], tiny[:, 0:1],
                                                                 AF.Sqrt, bias=0.0, scale=1.0),
                                waits=[("vector", c_mh)], inc=1)
                c_mh3 = pl.emit("vector", lambda v: v.tensor_scalar(tinyb[:, 0:1], tiny[:, 2:3],
                                                                    -0.125, None, ALU.mult),
                                waits=[("scalar", c_mh2)], inc=1)
                def fmh4(t):
                    return t.matmul(PS[6][:, 0:1], sb["onesrow_b"][:], tinyb[:, 0:1], start=True, stop=True)
                c_mh4 = pl.emit("tensor", fmh4, waits=[("vector", c_mh3)], inc=1)
                def fmh5(v):
                    return v.tensor_copy(biasA[:], PS[6][:, 0:1])
                c_mh5 = pl.emit("vector", fmh5, waits=[("tensor", c_mh4)], inc=1)
                pl.emit("tensor", None, waits=[("vector", c_mh5)])
                pl.emit("scalar", None, waits=[("vector", c_mh5)])
                if os.environ.get("KATTNCUT", "") == "norm":
                    pl.barrier()
                    continue

                def pt_base(sl):
                    return A_PT + sl * 1024 if sl < 2 else A_XN2 + (sl - 2) * 1024
                avlast = {0: 0, 1: 0, 2: 0, 3: 0}
                prev_f4 = [0, 0]
                prev_f5 = [0, 0]
                prev_f6 = [0, 0]
                c_f3_last = 0
                prev_exp = {0: 0, 1: 0}
                for qi in range(NTC):
                    nki = 4 * qi + 4
                    exp_done = {}
                    av_done = {}
                    msk_done = {}
                    for kit in range(nki + 2):
                        ki = kit
                        if ki < nki:
                            par = ki % 2
                            sl = ki % 4
                            def fsa(t, ki=ki, qi=qi, par=par):
                                return t.matmul(PSW[par][:, 0:512], ar(A_KR + ki * 128, 128),
                                                ar(A_QA + qi * TC, TC), start=True, stop=True,
                                                skip_group_check=True)
                            w_sa = []
                            if ki - 2 in exp_done:
                                w_sa.append(("scalar", exp_done[ki - 2]))
                            else:
                                w_sa.append(("scalar", prev_exp[par]))
                            c_sa = pl.emit("tensor", fsa, waits=w_sa, inc=1)
                            def fsb_(t, ki=ki, qi=qi, par=par):
                                return t.matmul(PSW[par][:, 512:1024], ar(A_KR + ki * 128, 128),
                                                ar(A_QB + qi * TC, TC), start=True, stop=True,
                                                skip_group_check=True)
                            c_sb = pl.emit("tensor", fsb_, waits=[], inc=1)
                            dj = ki - 4 * qi
                            def fex(s, par=par, sl=sl):
                                return s.activation(ar(pt_base(sl), 1024), PSW[par][:, 0:1024], AF.Exp,
                                                    bias=biasA[:], scale=1.0)
                            w_ex = [("tensor", c_sb)]
                            if ki - 4 in av_done:
                                w_ex.append(("tensor", av_done[ki - 4]))
                            else:
                                w_ex.append(("tensor", avlast[sl]))
                            c_ex = pl.emit("scalar", fex, waits=w_ex, inc=1)
                            exp_done[ki] = c_ex
                            prev_exp[par] = c_ex
                            if dj >= 0:
                                def fms(v, dj=dj, sl=sl):
                                    v.tensor_tensor(ar(pt_base(sl), TC), ar(pt_base(sl), TC),
                                                    ar(A_DMSK + dj * 512, TC), ALU.mult)
                                    return v.tensor_tensor(ar(pt_base(sl) + 512, TC), ar(pt_base(sl) + 512, TC),
                                                           ar(A_DMSK + dj * 512, TC), ALU.mult)
                                msk_done[ki] = pl.emit("vector", fms, waits=[("scalar", c_ex)], inc=1)
                        ka = kit - 2
                        if ka >= 0 and ka < nki:
                            sla = ka % 4
                            if ka in msk_done:
                                dep = [("vector", msk_done[ka])]
                            else:
                                dep = [("scalar", exp_done[ka])]
                            if ka == 0:
                                dep.append(("vector", prev_f5[0]))
                                dep.append(("vector", prev_f5[1]))
                            def fava(t, ka=ka, sla=sla, nki=nki):
                                return t.matmul(PS[4][0:65, :], ar(A_VT + ka * VTS, 65),
                                                ar(pt_base(sla), TC), start=(ka == 0), stop=(ka == nki - 1),
                                                skip_group_check=True)
                            pl.emit("tensor", fava, waits=dep, inc=1)
                            def favb(t, ka=ka, sla=sla, nki=nki):
                                return t.matmul(PS[5][0:65, :], ar(A_VT + ka * VTS + 66, 65),
                                                ar(pt_base(sla) + 512, TC), start=(ka == 0), stop=(ka == nki - 1),
                                                skip_group_check=True)
                            av_done[ka] = pl.emit("tensor", favb, waits=[], inc=1)
                    for sl in range(4):
                        kk = nki - 1 - ((nki - 1 - sl) % 4)
                        if kk in av_done:
                            avlast[sl] = av_done[kk]
                    for h in range(2):
                        def ffin1(s, h=h):
                            s.activation(PS[6 + h][64:65, :], PS[4 + h][64:65, :], AF.Ln, bias=0.0, scale=1.0)
                            return s.activation(PS[4 + h][64:65, :], PS[6 + h][64:65, :], AF.Exp,
                                                bias=0.0, scale=-1.0)
                        c_f1 = pl.emit("scalar", ffin1, waits=[("tensor", av_done[nki - 1]),
                                                               ("vector", prev_f4[h])], inc=1)
                        def ffin2(v, h=h):
                            return v.tensor_copy(ar(A_RROW, TC, (64, 65)), PS[4 + h][64:65, :])
                        c_f2 = pl.emit("vector", ffin2, waits=[("scalar", c_f1),
                                                               ("tensor", c_f3_last)], inc=1)
                        def ffin3(t, h=h):
                            return t.matmul(PS[6 + h][:, :], sb["ones2d"][64:65, 0:128],
                                            ar(A_RROW, TC, (64, 65)), start=True, stop=True)
                        c_f3 = pl.emit("tensor", ffin3, waits=[("vector", c_f2)], inc=1)
                        c_f3_last = c_f3
                        def ffin4(v, h=h):
                            return v.tensor_copy(ar(A_FRBC + h * 512, TC), PS[6 + h][:, :])
                        c_f4 = pl.emit("vector", ffin4, waits=[("tensor", c_f3)], inc=1)
                        prev_f4[h] = c_f4
                        def ffin5(v, h=h):
                            return v.tensor_mul(ar(A_FOUT + h * 512, TC, (0, 64)),
                                                PS[4 + h][0:64, :], ar(A_FRBC + h * 512, TC, (0, 64)))
                        c_f5 = pl.emit("vector", ffin5, waits=[("sync", prev_f6[h])], inc=1)
                        prev_f5[h] = c_f5
                        def ffin6(s_, h=h, qi=qi, hp=hp):
                            return s_.dma_start(out=attn_h[hp][64 * h:64 * h + 64, qi * TC:(qi + 1) * TC],
                                                in_=ar(A_FOUT + h * 512, TC, (0, 64)))
                        prev_f6[h] = pl.emit("sync", ffin6, waits=[("vector", c_f5)], inc=16)
                pl.barrier()
            if os.environ.get("KATTNCUT", "") == "flash":
                return
            def fwo(g):
                return g.dma_start(out=ar(A_WO, 8192).rearrange("p (kt c) -> p kt c", c=1024),
                                   in_=woT.rearrange("(kt p) c -> p kt c", p=128))
            c_wo = pl.emit("gpsimd", fwo, inc=16)
            ra_hist = {}
            och_hist = {}
            for tc in range(NTC):
                par = tc % 2
                def fas(s, tc=tc, par=par):
                    return s.dma_start(out=ar(A_AST + par * 4096, 4096)
                                       .rearrange("p (kt t) -> p kt t", t=TC),
                                       in_=attn_h.rearrange("hp p q -> p hp q")[:, :, tc * TC:(tc + 1) * TC])
                w_as = [("tensor", och_hist.get(tc - 2, 0))]
                c_as = pl.emit("sync", fas, waits=w_as, inc=16)
                for ht in range(NKT):
                    ob = PS[ht % 2]
                    def fo(t, ht=ht, par=par, ob=ob):
                        for kt in range(NKT):
                            mm = t.matmul(ob[:, :], ar(A_WO + kt * 1024 + ht * 128, 128),
                                          ar(A_AST + par * 4096 + kt * TC, TC),
                                          start=(kt == 0), stop=(kt == NKT - 1))
                        return mm
                    w_o = [("sync", c_as), ("gpsimd", c_wo)]
                    if (tc, ht - 2) in ra_hist:
                        w_o.append(("vector", ra_hist[(tc, ht - 2)]))
                    elif (tc - 1, 6 + ht) in ra_hist:
                        w_o.append(("vector", ra_hist[(tc - 1, 6 + ht)]))
                    c_o = pl.emit("tensor", fo, waits=w_o, inc=1)
                    och_hist[tc] = c_o
                    def fra(v, ht=ht, tc=tc, ob=ob):
                        return v.tensor_add(Rsl(ht, tc * TC, TC), Rsl(ht, tc * TC, TC), ob[:, :])
                    ra_hist[(tc, ht)] = pl.emit("vector", fra, waits=[("tensor", c_o)], inc=1)
            pl.barrier()

        def final_store():
            c = pl.emit("sync", lambda s: s.dma_start(
                out=outT.rearrange("(kt p) t -> p kt t", p=128),
                in_=R[:].rearrange("p (kt t) -> p kt t", t=T)), inc=16)
            pl.emit("sync", None, waits=[("sync", c)])

        init_phase()
        ln_phase(routing=True)
        expert_phase()
        if stage in ("attn", "full"):
            ln_phase(routing=False)
            if os.environ.get("KATTNCUT", "") != "ln2":
                attn_phase()
        if stage == "full":
            ln_phase(routing=True)
            expert_phase()
        final_store()

        def realize(eng_name):
            def runner(eng):
                for ws, fn, inc in pl.ops[eng_name]:
                    for we, wv in ws:
                        eng.wait_ge(sem[we], wv)
                    if fn is not None:
                        inst = fn(eng)
                        if inc:
                            inst.then_inc(sem[eng_name], inc)
            return runner

        @block.sync
        def _(sync):
            realize("sync")(sync)

        @block.gpsimd
        def _(gpsimd):
            realize("gpsimd")(gpsimd)

        @block.vector
        def _(vector):
            realize("vector")(vector)

        @block.scalar
        def _(scalar):
            realize("scalar")(scalar)

        @block.tensor
        def _(tensor):
            realize("tensor")(tensor)

    return nc


def kernel(L0_attn_x, L0_attn_wq, L0_attn_wk, L0_attn_wv, L0_attn_wo,
           L0_ffn_gate_w, L0_ffn_sh_gate, L0_ffn_sh_up, L0_ffn_sh_down,
           L0_ffn_rt_gate, L0_ffn_rt_up, L0_ffn_rt_down):
    stage = os.environ.get("KSTAGE", "full")
    debug = os.environ.get("KDEBUG", "") == "1"
    x = np.asarray(L0_attn_x, dtype=np.float32)
    f32c = lambda a: np.ascontiguousarray(np.asarray(a, dtype=np.float32))
    shared = {
        "wqT": f32c(np.asarray(L0_attn_wq).T), "wkT": f32c(np.asarray(L0_attn_wk).T),
        "wvT": f32c(np.asarray(L0_attn_wv).T), "woT": f32c(np.asarray(L0_attn_wo).T),
        "gw": f32c(L0_ffn_gate_w),
        "shg": f32c(L0_ffn_sh_gate), "shu": f32c(L0_ffn_sh_up), "shd": f32c(L0_ffn_sh_down),
        "rtg": f32c(L0_ffn_rt_gate), "rtu": f32c(L0_ffn_rt_up), "rtd": f32c(L0_ffn_rt_down),
    }
    shared.update(host_constants())
    nc = build_program(stage=stage, debug=debug)
    in_maps = []
    for c in range(8):
        m = dict(shared)
        m["xT"] = np.ascontiguousarray(x[c].T)
        in_maps.append(m)
    res = run_bass_kernel_spmd(nc, in_maps, core_ids=list(range(8)))
    if debug:
        kernel._dbg = res.results
    out = np.stack([np.ascontiguousarray(r["outT"].T) for r in res.results])
    return out


# version 06
# speedup vs baseline: 1.1898x; 1.1898x over previous
import os
import numpy as np
import ml_dtypes
import concourse.bass as bass
import concourse.mybir as mybir
from concourse.bass_utils import run_bass_kernel_spmd
from contextlib import ExitStack

F32 = mybir.dt.float32
BF16 = mybir.dt.bfloat16
AF = mybir.ActivationFunctionType
ALU = mybir.AluOpType
AX = mybir.AxisListType

T = 4096
D = 1024
I2 = 2048
NKT = 8
NTC = 8
TC = 512
NE = 8
NHP = 8
EPS = 1e-5

ENGS = ("sync", "gpsimd", "vector", "scalar", "tensor")

AR_N = 39808

A_WG = 0
A_WU = 8192
A_WD = 16384
A_XS = 24576
A_ACT = 32768

A_F32SCR = 0
A_XNCH = 8192
A_XSE = 12288
A_SQ = 16384
A_ROWS = 20480
A_LOGSB = 27648
A_LROW = 0
A_SEROW = 28672
A_XSE1 = 32768

A_QA = 0
A_QB = 4096
A_KR = 8192
A_VT = 12288
VTS = 132
A_ROPC = 16512
A_ROPS = 20608
A_DMSK = 24704
A_XN2 = 26752
A_QRAW = 30848
A_KRAW = 31360
A_VTMP = 31872
A_SCA = 32384
A_SCB = 32896
A_PT = 33408
A_RROW = 35456
A_STAT = 35968
A_FRBC = 31872
A_FOUT = 30848

A_WO = 0
A_AST = 8192


class Plan:
    def __init__(self):
        self.ops = {e: [] for e in ENGS}
        self.n = {e: 0 for e in ENGS}

    def emit(self, e, fn=None, waits=(), inc=0, bump=0):
        ws = tuple((we, wv) for (we, wv) in waits if wv > 0)
        self.ops[e].append((ws, fn, inc))
        self.n[e] += inc + bump
        return self.n[e]

    def barrier(self):
        snap = dict(self.n)
        for e in ENGS:
            self.emit(e, None, waits=[(o, snap[o]) for o in ENGS if o != e])
        return snap


def _bf(x):
    return np.ascontiguousarray(np.asarray(x).astype(ml_dtypes.bfloat16))


_HC = None


def host_constants():
    global _HC
    if _HC is not None:
        return _HC
    theta = 10000.0
    pos = np.arange(T, dtype=np.float64)
    kk = np.arange(16, dtype=np.float64)
    k32 = np.arange(32, dtype=np.float64)
    angle = pos[None, :] / theta ** (2.0 * k32[:, None] / 64.0)
    sin, cos = np.sin(angle), np.cos(angle)
    ctab = np.zeros((128, T))
    stab = np.zeros((128, T))
    for f in range(64):
        ctab[f] = cos[f // 2]
        ctab[64 + f] = cos[f // 2]
        s = -sin[f // 2] if f % 2 == 0 else sin[f // 2]
        stab[f] = s
        stab[64 + f] = s
    swap = np.zeros((128, 128))
    for r in range(0, 128, 2):
        swap[r + 1, r] = 1.0
        swap[r, r + 1] = 1.0
    dm = np.zeros((128, 4, 512))
    for j in range(4):
        dm[:, j, :] = (128 * j + np.arange(128)[:, None] <= np.arange(512)[None, :])
    _HC = dict(
        ropec=_bf(ctab), ropes=_bf(stab), swapm=_bf(swap), identb=_bf(np.eye(128)),
        dmask=_bf(dm.reshape(128, 2048)), ones2d=_bf(np.ones((128, 128))),
        onescol_f=np.ones((128, 1), dtype=np.float32),
        onescol_b=_bf(np.ones((128, 1))),
        onesrow_f=np.ones((1, 128), dtype=np.float32),
        onesrow_b=_bf(np.ones((1, 128))),
        onesa=_bf(np.concatenate([np.ones((64, 1)), np.zeros((64, 1))])),
        onesb_=_bf(np.concatenate([np.zeros((64, 1)), np.ones((64, 1))])),
    )
    return _HC


def build_program(stage="full", debug=False):
    nc = bass.Bass("TRN2", target_bir_lowering=False)
    din = lambda nm, shp, dt: nc.dram_tensor(nm, shp, dt, kind="ExternalInput").ap()
    xT = din("xT", [D, T], F32)
    wqT = din("wqT", [D, D], F32)
    wkT = din("wkT", [D, D], F32)
    wvT = din("wvT", [D, D], F32)
    woT = din("woT", [D, D], F32)
    gw = din("gw", [D, NE], F32)
    shg = din("shg", [D, I2], F32)
    shu = din("shu", [D, I2], F32)
    shd = din("shd", [I2, D], F32)
    rtg = din("rtg", [NE, D, I2], F32)
    rtu = din("rtu", [NE, D, I2], F32)
    rtd = din("rtd", [NE, I2, D], F32)
    hc = host_constants()
    cst = {k: din(k, list(v.shape), BF16 if v.dtype == ml_dtypes.bfloat16 else F32)
           for k, v in hc.items()}
    outT = nc.dram_tensor("outT", [D, T], F32, kind="ExternalOutput").ap()
    dbg_kind = "ExternalOutput" if debug else "Internal"
    xn_h = nc.dram_tensor("xn_h", [128, NKT, T], BF16, kind=dbg_kind).ap()
    xs_h = nc.dram_tensor("xs_h", [NE, 128, NKT, T], BF16, kind=dbg_kind).ap()
    attn_h = nc.dram_tensor("attn_h", [NHP, 128, T], BF16, kind=dbg_kind).ap()

    with ExitStack() as ctx:
        E = ctx.enter_context
        R = E(nc.sbuf_tensor("R", [128, NKT * T], F32))
        AR = E(nc.sbuf_tensor("AR", [128, AR_N], BF16))
        ARF = AR.bitcast(F32)
        sb = {}
        for k, v in hc.items():
            if k in ("ropec", "ropes", "dmask"):
                continue
            dt = BF16 if v.dtype == ml_dtypes.bfloat16 else F32
            sb[k] = E(nc.sbuf_tensor("c_" + k, list(v.shape), dt))
        gwsb = E(nc.sbuf_tensor("gwsb", [128, NKT, NE], F32))
        biasA = E(nc.sbuf_tensor("biasA", [128, 1], F32))
        biasB = E(nc.sbuf_tensor("biasB", [128, 1], F32))
        runm = E(nc.sbuf_tensor("runm", [1, 8], F32))
        tiny = E(nc.sbuf_tensor("tiny", [1, 8], F32))
        tinyb = E(nc.sbuf_tensor("tinyb", [1, 8], BF16))
        PSW = [E(nc.psum_tensor(f"PSW{i}", [128, 1024], F32)) for i in range(2)]
        PS47 = [E(nc.psum_tensor(f"PS{i}", [128, 512], F32)) for i in range(4, 8)]
        PS = [PSW[0][:, 0:512], PSW[0][:, 512:1024],
              PSW[1][:, 0:512], PSW[1][:, 512:1024]] + [p[:] for p in PS47]
        PSB = [None, None, None, None] + [p.bitcast(BF16)[:] for p in PS47]
        PSBW0 = PSW[0].bitcast(BF16)
        sem = {e: E(nc.semaphore("s_" + e)) for e in ENGS}
        block = E(nc.Block())

        pl = Plan()

        def Rsl(kt, lo, ln):
            return R[:, kt * T + lo: kt * T + lo + ln]

        def ar(lo, ln, part=(0, 128)):
            return AR[part[0]:part[1], lo:lo + ln]

        def arf(lo_bf, ln, part=(0, 128)):
            assert lo_bf % 2 == 0
            return ARF[part[0]:part[1], lo_bf // 2: lo_bf // 2 + ln]

        def init_phase():
            pl.emit("sync", lambda s: s.dma_start(
                out=R[:].rearrange("p (kt t) -> p kt t", t=T),
                in_=xT.rearrange("(kt p) t -> p kt t", p=128)), inc=16)
            for k in hc:
                if k in ("ropec", "ropes", "dmask"):
                    continue
                pl.emit("sync", (lambda k: lambda s: s.dma_start(out=sb[k][:], in_=cst[k]))(k), inc=16)
            pl.emit("sync", lambda s: s.dma_start(
                out=gwsb[:], in_=gw.rearrange("(kt p) e -> p kt e", p=128)), inc=16)
            pl.barrier()

        def ln_phase(routing):
            r_mean = arf(A_ROWS, 512, (0, 1))
            r_B = arf(A_ROWS + 1024, 512, (0, 1))
            r_var = arf(A_ROWS + 2048, 512, (0, 1))
            r_lnv = r_B
            r_rstd = arf(A_ROWS + 3072, 512, (0, 1))
            r_top = arf(A_ROWS + 4096, 512, (0, 1))
            r_sco = arf(A_ROWS + 5120, 512, (0, 1))
            r_tmp = arf(A_ROWS + 6144, 512, (0, 1))
            lrow = [arf(A_LROW + e * 1024, 512, (0, 1)) for e in range(NE)]
            se_rows = [ar(A_SEROW + e * TC, TC, (0, 1)) for e in range(NE)]
            xse_slot = [A_XSE, A_XSE1]
            c_ssq_prev = 0
            c_xndma_prev = 0
            c_rd_prev = 0
            c_xsd_hist = {}
            c_row1_hist = {}
            c_lg_prev = 0
            c_bc_prev = 0
            c_bcse_prev = 0
            c_xs_prev = 0
            nxs = 0
            pending_xs = None
            for tc in range(NTC):
                if pending_xs is not None:
                    pending_xs()
                    pending_xs = None
                lo = tc * TC
                sB = PS[0] if tc % 2 == 0 else PS[6]
                qB = PS[1] if tc % 2 == 0 else PS[7]
                def fsum(t, lo=lo, sB=sB):
                    for kt in range(NKT):
                        mm = t.matmul(sB[0:1, :], sb["onescol_f"][:], Rsl(kt, lo, TC),
                                      start=(kt == 0), stop=(kt == NKT - 1))
                    return mm
                c_sum = pl.emit("tensor", fsum,
                                waits=[("vector", c_row1_hist.get(tc - 2, 0))], inc=1)
                def fsq(s, lo=lo):
                    return s.activation(
                        ar(A_SQ, 4096).rearrange("p (kt t) -> p kt t", t=TC),
                        R[:].rearrange("p (kt t) -> p kt t", t=T)[:, :, lo:lo + TC],
                        AF.Square, bias=0.0, scale=1.0)
                c_sq = pl.emit("scalar", fsq, waits=[("tensor", c_ssq_prev)], inc=1)
                def fssq(t, qB=qB):
                    for kt in range(NKT):
                        mm = t.matmul(qB[0:1, :], sb["onescol_b"][:], ar(A_SQ + kt * TC, TC),
                                      start=(kt == 0), stop=(kt == NKT - 1))
                    return mm
                c_ssq = pl.emit("tensor", fssq, waits=[("scalar", c_sq)], inc=1)
                c_ssq_prev = c_ssq
                def frow1(v, sB=sB, qB=qB):
                    v.tensor_scalar(r_mean, sB[0:1, :], 1.0 / D, None, ALU.mult)
                    v.tensor_scalar(r_var, qB[0:1, :], 1.0 / D, EPS, ALU.mult, ALU.add)
                    v.tensor_tensor(r_tmp, r_mean, r_mean, ALU.mult)
                    return v.tensor_sub(r_var, r_var, r_tmp)
                c_row1 = pl.emit("vector", frow1, waits=[("tensor", c_ssq)], inc=1)
                c_row1_hist[tc] = c_row1
                def frow2(s):
                    s.activation(r_lnv, r_var, AF.Ln, bias=0.0, scale=1.0)
                    return s.activation(r_rstd, r_lnv, AF.Exp, bias=0.0, scale=-0.5)
                c_row2 = pl.emit("scalar", frow2, waits=[("vector", c_row1),
                                                         ("tensor", c_bc_prev)], inc=1)
                def frow3(v):
                    return v.tensor_tensor(r_B, r_mean, r_rstd, ALU.mult)
                c_row3 = pl.emit("vector", frow3, waits=[("scalar", c_row2)], inc=1)
                def fbc(t):
                    t.matmul(PS[2][:, :], sb["onesrow_f"][:], r_rstd, start=True, stop=True)
                    return t.matmul(PS[3][:, :], sb["onesrow_f"][:], r_B, start=True, stop=True)
                c_bc = pl.emit("tensor", fbc, waits=[("vector", c_row3)], inc=1)
                c_bc_prev = c_bc
                def fxn(v, lo=lo):
                    for kt in range(NKT):
                        v.tensor_tensor(arf(A_F32SCR + kt * 1024, TC), Rsl(kt, lo, TC),
                                        PS[2][:, :], ALU.mult)
                    for kt in range(NKT):
                        last = v.tensor_sub(arf(A_F32SCR + kt * 1024, TC),
                                            arf(A_F32SCR + kt * 1024, TC), PS[3][:, :])
                    return last
                c_xnf = pl.emit("vector", fxn, waits=[("tensor", c_bc),
                                                      ("tensor", c_lg_prev)], inc=1)
                def fxnb(v):
                    return v.tensor_copy(ar(A_XNCH, 4096), arf(A_F32SCR, 4096))
                c_xnb = pl.emit("vector", fxnb, waits=[("sync", c_xndma_prev)], inc=1)
                def fxnd(s, tc=tc):
                    return s.dma_start(out=xn_h[:, :, tc * TC:(tc + 1) * TC],
                                       in_=ar(A_XNCH, 4096).rearrange("p (kt t) -> p kt t", t=TC))
                c_xndma = pl.emit("sync", fxnd, waits=[("vector", c_xnb)], inc=16)
                c_xndma_prev = c_xndma

                if routing:
                    def flg(t):
                        for kt in range(NKT):
                            mm = t.matmul(PS[4][0:8, :], gwsb[:, kt, :], arf(A_F32SCR + kt * 1024, TC),
                                          start=(kt == 0), stop=(kt == NKT - 1))
                        return mm
                    c_lg = pl.emit("tensor", flg, waits=[("vector", c_xnf)], inc=1)
                    c_lg_prev = c_lg
                    def flev(v):
                        return v.tensor_copy(arf(A_LOGSB, 512, (0, 8)), PS[4][0:8, :])
                    c_lev = pl.emit("vector", flev, waits=[("tensor", c_lg),
                                                           ("sync", c_rd_prev)], inc=1)
                    def frd(s):
                        for e in range(NE - 1):
                            s.dma_start(out=lrow[e], in_=ARF[e:e + 1, A_LOGSB // 2: A_LOGSB // 2 + 512]).then_inc(sem["sync"], 16)
                        return s.dma_start(out=lrow[NE - 1], in_=ARF[NE - 1:NE, A_LOGSB // 2: A_LOGSB // 2 + 512])
                    c_rd = pl.emit("sync", frd, waits=[("vector", c_lev)], inc=16, bump=16 * (NE - 1))
                    c_rd_prev = c_rd
                    def ftop(v):
                        v.tensor_tensor(r_top, lrow[0], lrow[1], ALU.max)
                        for e in range(2, NE):
                            last = v.tensor_tensor(r_top, r_top, lrow[e], ALU.max)
                        return last
                    c_top = pl.emit("vector", ftop, waits=[("sync", c_rd)], inc=1)
                    def fsco(s):
                        return s.activation(r_sco, r_top, AF.Sigmoid, bias=0.0, scale=1.0)
                    c_sco = pl.emit("scalar", fsco, waits=[("vector", c_top)], inc=1)
                    def fseb(v):
                        for e in range(NE):
                            v.tensor_tensor(r_tmp, lrow[e], r_top, ALU.is_ge)
                            last = v.tensor_tensor(se_rows[e], r_tmp, r_sco, ALU.mult)
                        return last
                    c_seb = pl.emit("vector", fseb, waits=[("scalar", c_sco),
                                                           ("tensor", c_bcse_prev)], inc=1)

                    def mk_xs(tc=tc, c_seb=c_seb):
                        def run():
                            nonlocal c_bcse_prev, c_xs_prev, nxs
                            c_xs_hist = {}
                            for e in range(NE):
                                bcb = PS[2] if e % 2 == 0 else PS[5]
                                def fbcse(t, e=e, bcb=bcb):
                                    return t.matmul(bcb[:, :], sb["onesrow_b"][:], se_rows[e],
                                                    start=True, stop=True)
                                c_bcse = pl.emit("tensor", fbcse, waits=[("vector", c_seb),
                                                                         ("vector", c_xs_hist.get(e - 2, 0))], inc=1)
                                c_bcse_prev = c_bcse
                                slot = xse_slot[nxs % 2]
                                def fxs(v, slot=slot, bcb=bcb):
                                    for kt in range(NKT):
                                        last = v.tensor_tensor(ar(slot + kt * TC, TC),
                                                               ar(A_XNCH + kt * TC, TC),
                                                               bcb[:, :], ALU.mult)
                                    return last
                                c_xs = pl.emit("vector", fxs, waits=[("tensor", c_bcse),
                                                                     ("sync", c_xsd_hist.get(nxs - 2, 0))], inc=1)
                                c_xs_hist[e] = c_xs
                                c_xs_prev = c_xs
                                def fxsd(s, e=e, tc=tc, slot=slot):
                                    return s.dma_start(out=xs_h[e][:, :, tc * TC:(tc + 1) * TC],
                                                       in_=ar(slot, 4096).rearrange("p (kt t) -> p kt t", t=TC))
                                c_xsd_hist[nxs] = pl.emit("sync", fxsd, waits=[("vector", c_xs)], inc=16)
                                nxs += 1
                        return run
                    pending_xs = mk_xs()
            if pending_xs is not None:
                pending_xs()
                pending_xs = None
            pl.barrier()

        def expert_phase():
            units = [(e, h) for e in range(NE + 1) for h in range(2)]
            gtc = 0
            pe_after_gu = pl.n["tensor"]
            pe_after_dn = pl.n["tensor"]
            radd_last = pl.n["vector"]
            pending_down = None
            gu_hist = {}
            for ui, (e, ih) in enumerate(units):
                if pending_down is not None:
                    pe_after_dn, radd_last = pending_down()
                    pending_down = None
                if e < NE:
                    srcg, srcu, srcd = rtg[e], rtu[e], rtd[e]
                else:
                    srcg, srcu, srcd = shg, shu, shd
                def fwgu(g, srcg=srcg, srcu=srcu, ih=ih):
                    g.dma_start(out=ar(A_WG, 8192).rearrange("p (kt i) -> p kt i", i=1024),
                                in_=srcg.rearrange("(kt p) i -> p kt i", p=128)[:, :, ih * 1024:(ih + 1) * 1024]).then_inc(sem["gpsimd"], 16)
                    return g.dma_start(out=ar(A_WU, 8192).rearrange("p (kt i) -> p kt i", i=1024),
                                       in_=srcu.rearrange("(kt p) i -> p kt i", p=128)[:, :, ih * 1024:(ih + 1) * 1024])
                c_wgu = pl.emit("gpsimd", fwgu, waits=[("tensor", pe_after_gu)], inc=16, bump=16)
                def fwd(g, srcd=srcd, ih=ih):
                    return g.dma_start(out=ar(A_WD, 8192).rearrange("p (kt h) -> p kt h", h=1024),
                                       in_=srcd.rearrange("(ikt p) h -> p ikt h", p=128)[:, 8 * ih:8 * ih + 8, :])
                c_wd = pl.emit("gpsimd", fwd, waits=[("tensor", pe_after_dn)], inc=16)

                silu_done = {}
                ustt_done = {}
                for tc in range(NTC):
                    if pending_down is not None:
                        pe_after_dn, radd_last = pending_down()
                        pending_down = None
                    par = gtc % 2
                    src_h = xs_h[e] if e < NE else xn_h
                    def fxs(s, src_h=src_h, tc=tc, par=par):
                        return s.dma_start(out=ar(A_XS + par * 4096, 4096)
                                           .rearrange("p (kt t) -> p kt t", t=TC),
                                           in_=src_h[:, :, tc * TC:(tc + 1) * TC])
                    xs_w = [("tensor", gu_hist.get(gtc - 2, 0))]
                    c_xsl = pl.emit("sync", fxs, waits=xs_w, inc=16)
                    for it in range(NKT):
                        gb = PS[it % 2]
                        ub = PS[2 + it % 2]
                        def fg(t, it=it, par=par, gb=gb):
                            for kt in range(NKT):
                                mm = t.matmul(gb[:, :], ar(A_WG + kt * 1024 + it * 128, 128),
                                              ar(A_XS + par * 4096 + kt * TC, TC),
                                              start=(kt == 0), stop=(kt == NKT - 1))
                            return mm
                        w_g = [("gpsimd", c_wgu), ("sync", c_xsl)]
                        if (tc, it - 2) in silu_done:
                            w_g.append(("scalar", silu_done[(tc, it - 2)]))
                        elif (tc - 1, 6 + it) in silu_done:
                            w_g.append(("scalar", silu_done[(tc - 1, 6 + it)]))
                        c_g = pl.emit("tensor", fg, waits=w_g, inc=1)
                        def fu(t, it=it, par=par, ub=ub):
                            for kt in range(NKT):
                                mm = t.matmul(ub[:, :], ar(A_WU + kt * 1024 + it * 128, 128),
                                              ar(A_XS + par * 4096 + kt * TC, TC),
                                              start=(kt == 0), stop=(kt == NKT - 1))
                            return mm
                        w_u = []
                        if (tc, it - 2) in ustt_done:
                            w_u.append(("vector", ustt_done[(tc, it - 2)]))
                        elif (tc - 1, 6 + it) in ustt_done:
                            w_u.append(("vector", ustt_done[(tc - 1, 6 + it)]))
                        c_u = pl.emit("tensor", fu, waits=w_u, inc=1)
                        gu_hist[gtc] = c_u
                        def fsl(s, it=it, gb=gb):
                            return s.activation(ar(A_ACT + it * TC, TC), gb[:, :], AF.Silu,
                                                bias=0.0, scale=1.0)
                        silu_done[(tc, it)] = pl.emit("scalar", fsl, waits=[("tensor", c_g)], inc=1)
                        def fus(v, it=it, ub=ub):
                            return v.scalar_tensor_tensor(ar(A_ACT + it * TC, TC), ub[:, :], 0.5,
                                                          ar(A_ACT + it * TC, TC), ALU.mult, ALU.mult)
                        ustt_done[(tc, it)] = pl.emit("vector", fus,
                                                      waits=[("tensor", c_u),
                                                             ("scalar", silu_done[(tc, it)])], inc=1)
                    pe_after_gu = gu_hist[gtc]
                    def mk_down(tc=tc, ustt_val=ustt_done[(tc, NKT - 1)], c_wd=c_wd):
                        def run():
                            last_pe = 0
                            last_radd = radd_last
                            for tq in range(2):
                                y_done = None
                                for ht in range(NKT):
                                    yb = PS[4 + ht % 4]
                                    ycols = (ht // 4) * 256
                                    def fd(t, ht=ht, tq=tq, tc=tc, yb=yb, ycols=ycols):
                                        for ikt in range(NKT):
                                            mm = t.matmul(yb[:, ycols:ycols + 256],
                                                          ar(A_WD + ikt * 1024 + ht * 128, 128),
                                                          ar(A_ACT + ikt * TC + tq * 256, 256),
                                                          start=(ikt == 0), stop=(ikt == NKT - 1))
                                        return mm
                                    wd_w = [("gpsimd", c_wd), ("vector", ustt_val)]
                                    if ht == 0:
                                        wd_w.append(("vector", last_radd))
                                    y_done = pl.emit("tensor", fd, waits=wd_w, inc=1)
                                def fra(v, tq=tq, tc=tc):
                                    for ht in range(NKT):
                                        last = v.tensor_add(Rsl(ht, tc * TC + tq * 256, 256),
                                                            Rsl(ht, tc * TC + tq * 256, 256),
                                                            PS[4 + ht % 4][:, (ht // 4) * 256:(ht // 4) * 256 + 256])
                                    return last
                                last_radd = pl.emit("vector", fra, waits=[("tensor", y_done)], inc=1)
                                last_pe = y_done
                            return last_pe, last_radd
                        return run
                    pending_down = mk_down()
                    gtc += 1
            if pending_down is not None:
                pe_after_dn, radd_last = pending_down()
            pl.barrier()

        def attn_phase():
            def fld(s):
                s.dma_start(out=ar(A_ROPC, T), in_=cst["ropec"]).then_inc(sem["sync"], 16)
                s.dma_start(out=ar(A_ROPS, T), in_=cst["ropes"]).then_inc(sem["sync"], 16)
                return s.dma_start(out=ar(A_DMSK, 2048), in_=cst["dmask"])
            pl.emit("sync", fld, inc=16, bump=32)
            pl.barrier()
            for hp in range(NHP):
                def fwst(g, hp=hp):
                    g.dma_start(out=ar(A_STAT, 1024).rearrange("p (kt c) -> p kt c", c=128),
                                in_=wqT.rearrange("(kt p) c -> p kt c", p=128)[:, :, hp * 128:(hp + 1) * 128]).then_inc(sem["gpsimd"], 16)
                    g.dma_start(out=ar(A_STAT + 1024, 1024).rearrange("p (kt c) -> p kt c", c=128),
                                in_=wkT.rearrange("(kt p) c -> p kt c", p=128)[:, :, hp * 128:(hp + 1) * 128]).then_inc(sem["gpsimd"], 16)
                    return g.dma_start(out=ar(A_STAT + 2048, 1024).rearrange("p (kt c) -> p kt c", c=128),
                                       in_=wvT.rearrange("(kt p) c -> p kt c", p=128)[:, :, hp * 128:(hp + 1) * 128])
                c_wst = pl.emit("gpsimd", fwst, waits=[("tensor", pl.n["tensor"])], inc=16, bump=32)
                def fz(v):
                    v.memset(ar(A_QA, T, (64, 128)), 0.0)
                    v.memset(ar(A_QB, T, (0, 64)), 0.0)
                    return v.memset(runm[:], -3.0e38)
                c_z = pl.emit("vector", fz, waits=[("tensor", pl.n["tensor"]),
                                                   ("scalar", pl.n["scalar"]),
                                                   ("sync", pl.n["sync"])], inc=1)
                ev = {"evQ": 0, "evK": 0, "vch": 0, "ropeK": 0}
                vtr = {}
                for c in range(NTC):
                    lo = c * TC
                    def fxnc(s, c=c):
                        return s.dma_start(out=ar(A_XN2, 4096).rearrange("p (kt t) -> p kt t", t=TC),
                                           in_=xn_h[:, :, c * TC:(c + 1) * TC])
                    c_xn = pl.emit("sync", fxnc, waits=[("tensor", ev["vch"])], inc=16)
                    def fq(t):
                        for kt in range(NKT):
                            mm = t.matmul(PS[6][:, :], ar(A_STAT + kt * 128, 128),
                                          ar(A_XN2 + kt * TC, TC), start=(kt == 0), stop=(kt == NKT - 1))
                        return mm
                    c_q = pl.emit("tensor", fq, waits=[("sync", c_xn), ("gpsimd", c_wst),
                                                       ("scalar", ev["evQ"]), ("vector", c_z)], inc=1)
                    def feq(s):
                        return s.mul(ar(A_QRAW, TC), PS[6][:, :], 0.125)
                    c_eq = pl.emit("scalar", feq, waits=[("tensor", c_q)], inc=1)
                    def fk(t):
                        for kt in range(NKT):
                            mm = t.matmul(PS[7][:, :], ar(A_STAT + 1024 + kt * 128, 128),
                                          ar(A_XN2 + kt * TC, TC), start=(kt == 0), stop=(kt == NKT - 1))
                        return mm
                    c_k = pl.emit("tensor", fk, waits=[("scalar", ev["evK"])], inc=1)
                    def fek(s):
                        return s.copy(ar(A_KRAW, TC), PS[7][:, :])
                    c_ek = pl.emit("scalar", fek, waits=[("tensor", c_k)], inc=1)
                    ev["evK"] = c_ek
                    def fv(t):
                        for kt in range(NKT):
                            mm = t.matmul(PS[6][:, :], ar(A_STAT + 2048 + kt * 128, 128),
                                          ar(A_XN2 + kt * TC, TC), start=(kt == 0), stop=(kt == NKT - 1))
                        return mm
                    c_v = pl.emit("tensor", fv, waits=[("scalar", c_eq)], inc=1)
                    ev["vch"] = c_v
                    def fev(s):
                        return s.copy(ar(A_VTMP, TC), PS[6][:, :])
                    c_ev = pl.emit("scalar", fev, waits=[("tensor", c_v)], inc=1)
                    ev["evQ"] = c_ev
                    NOROPE = os.environ.get("KNOROPE", "") == "1"
                    if NOROPE:
                        def fcpq(v, lo=lo):
                            v.tensor_copy(ar(A_QA + lo, TC, (0, 64)), ar(A_QRAW, TC, (0, 64)))
                            v.tensor_copy(ar(A_QB + lo, TC, (64, 128)), ar(A_QRAW, TC, (64, 128)))
                            return v.tensor_copy(ar(A_KR + lo, TC), ar(A_KRAW, TC))
                        c_rk = pl.emit("vector", fcpq, waits=[("scalar", c_ek),
                                                              ("vector", ev["ropeK"])], inc=1)
                        ev["ropeK"] = c_rk
                    def fsw(t):
                        return t.matmul(PS[5][:, :], sb["swapm"][:], ar(A_QRAW, TC),
                                        start=True, stop=True)
                    c_sw = (pl.n["tensor"] if NOROPE else
                            pl.emit("tensor", fsw, waits=[("scalar", c_eq),
                                                          ("vector", ev["ropeK"])], inc=1))
                    def frq(v, lo=lo):
                        v.tensor_tensor(ar(A_SCA, TC), ar(A_QRAW, TC), ar(A_ROPC + lo, TC), ALU.mult)
                        v.tensor_tensor(ar(A_SCB, TC), PS[5][:, :], ar(A_ROPS + lo, TC), ALU.mult)
                        v.tensor_add(ar(A_QA + lo, TC, (0, 64)), ar(A_SCA, TC, (0, 64)),
                                     ar(A_SCB, TC, (0, 64)))
                        return v.tensor_add(ar(A_QB + lo, TC, (64, 128)), ar(A_SCA, TC, (64, 128)),
                                            ar(A_SCB, TC, (64, 128)))
                    c_rq = 0 if NOROPE else pl.emit("vector", frq, waits=[("tensor", c_sw)], inc=1)
                    def fswk(t):
                        return t.matmul(PS[5][:, :], sb["swapm"][:], ar(A_KRAW, TC),
                                        start=True, stop=True)
                    c_swk = (pl.n["tensor"] if NOROPE else
                             pl.emit("tensor", fswk, waits=[("scalar", c_ek), ("vector", c_rq)], inc=1))
                    def frk(v, lo=lo):
                        v.tensor_tensor(ar(A_SCA, TC), ar(A_KRAW, TC), ar(A_ROPC + lo, TC), ALU.mult)
                        v.tensor_tensor(ar(A_SCB, TC), PS[5][:, :], ar(A_ROPS + lo, TC), ALU.mult)
                        return v.tensor_add(ar(A_KR + lo, TC), ar(A_SCA, TC), ar(A_SCB, TC))
                    if not NOROPE:
                        c_rk = pl.emit("vector", frk, waits=[("tensor", c_swk)], inc=1)
                        ev["ropeK"] = c_rk
                    for j in range(4):
                        KNV = os.environ.get("KNOVT", "")
                        if KNV == "1":
                            break
                        tj = 4 * c + j
                        def ftr(t, j=j):
                            tout = PSB[4][:, 0:128] if j % 2 == 0 else PSBW0[:, 0:128]
                            t.transpose(tout, ar(A_VTMP + j * 128, 128), sb["identb"][:])
                            return t.matmul(PS[0][0:1, 0:8], sb["onesrow_b"][0:1, 0:1],
                                            sb["onesrow_b"][0:1, 0:8], start=True, stop=True)
                        w_tr = [("scalar", c_ev)]
                        if (j - 2) in vtr:
                            w_tr.append(("vector", vtr[j - 2]))
                        c_tr = pl.emit("tensor", ftr, waits=w_tr, inc=1)
                        if KNV == "2":
                            def fcv(v, j=j, tj=tj):
                                return v.tensor_copy(ar(A_VT + tj * 130, 64), ar(A_VTMP, 64))
                        elif KNV == "3":
                            def fcv(v, j=j, tj=tj):
                                tin = PSB[4] if j % 2 == 0 else PSBW0
                                v.tensor_copy(ar(A_VT + tj * VTS, 64), tin[:, 0:64])
                                return v.tensor_copy(ar(A_VT + tj * VTS + 66, 64), tin[:, 64:128])
                        else:
                            def fcv(v, j=j, tj=tj):
                                tin = PSB[4] if j % 2 == 0 else PSBW0
                                v.tensor_copy(ar(A_VT + tj * VTS, 64), tin[:, 0:64])
                                v.memset(ar(A_VT + tj * VTS + 64, 2), 1.0)
                                v.tensor_copy(ar(A_VT + tj * VTS + 66, 64), tin[:, 64:128])
                                return v.memset(ar(A_VT + tj * VTS + 130, 2), 1.0)
                        vtr[j] = pl.emit("vector", fcv, waits=[("tensor", c_tr)], inc=1)
                    if 3 in vtr:
                        pl.emit("tensor", None, waits=[("vector", vtr[3])])
                if os.environ.get("KATTNCUT", "") == "proj":
                    pl.barrier()
                    continue
                c_rm = 0
                for c in range(NTC):
                    lo = c * TC
                    def fsq2(s, lo=lo):
                        s.activation(ar(A_QRAW, TC), ar(A_QA + lo, TC), AF.Square, bias=0.0, scale=1.0)
                        s.activation(ar(A_KRAW, TC), ar(A_QB + lo, TC), AF.Square, bias=0.0, scale=1.0)
                        return s.activation(ar(A_VTMP, TC), ar(A_KR + lo, TC), AF.Square, bias=0.0, scale=1.0)
                    c_s2 = pl.emit("scalar", fsq2, waits=[("vector", pl.n["vector"]),
                                                          ("tensor", pl.n["tensor"])], inc=1)
                    def fnm(t):
                        t.matmul(PS[3][0:1, :], sb["onescol_b"][:], ar(A_QRAW, TC), start=True, stop=True)
                        t.matmul(PS[6][0:1, :], sb["onescol_b"][:], ar(A_KRAW, TC), start=True, stop=True)
                        t.matmul(PS[7][0:1, :], sb["onesa"][:], ar(A_VTMP, TC), start=True, stop=True)
                        return t.matmul(PS[5][0:1, :], sb["onesb_"][:], ar(A_VTMP, TC), start=True, stop=True)
                    c_nm = pl.emit("tensor", fnm, waits=[("scalar", c_s2), ("vector", c_rm)], inc=1)
                    def frm(v):
                        v.reduce_max(tiny[:, 0:1], PS[3][0:1, :], axis=AX.X)
                        v.reduce_max(tiny[:, 1:2], PS[6][0:1, :], axis=AX.X)
                        v.reduce_max(tiny[:, 2:3], PS[7][0:1, :], axis=AX.X)
                        v.reduce_max(tiny[:, 3:4], PS[5][0:1, :], axis=AX.X)
                        return v.tensor_tensor(runm[:, 0:4], runm[:, 0:4], tiny[:, 0:4], ALU.max)
                    c_rm = pl.emit("vector", frm, waits=[("tensor", c_nm)], inc=1)
                def fmh(v):
                    v.tensor_tensor(tiny[:, 0:1], runm[:, 0:1], runm[:, 2:3], ALU.mult)
                    v.tensor_tensor(tiny[:, 1:2], runm[:, 1:2], runm[:, 3:4], ALU.mult)
                    return v.tensor_tensor(tiny[:, 0:1], tiny[:, 0:1], tiny[:, 1:2], ALU.max)
                c_mh = pl.emit("vector", fmh, waits=[("vector", c_rm)], inc=1)
                c_mh2 = pl.emit("scalar", lambda s: s.activation(tiny[:, 2:3], tiny[:, 0:1],
                                                                 AF.Sqrt, bias=0.0, scale=1.0),
                                waits=[("vector", c_mh)], inc=1)
                c_mh3 = pl.emit("vector", lambda v: v.tensor_scalar(tinyb[:, 0:1], tiny[:, 2:3],
                                                                    -0.125, None, ALU.mult),
                                waits=[("scalar", c_mh2)], inc=1)
                def fmh4(t):
                    return t.matmul(PS[6][:, 0:1], sb["onesrow_b"][:], tinyb[:, 0:1], start=True, stop=True)
                c_mh4 = pl.emit("tensor", fmh4, waits=[("vector", c_mh3)], inc=1)
                def fmh5(v):
                    return v.tensor_copy(biasA[:], PS[6][:, 0:1])
                c_mh5 = pl.emit("vector", fmh5, waits=[("tensor", c_mh4)], inc=1)
                pl.emit("tensor", None, waits=[("vector", c_mh5)])
                pl.emit("scalar", None, waits=[("vector", c_mh5)])
                if os.environ.get("KATTNCUT", "") == "norm":
                    pl.barrier()
                    continue

                def pt_base(sl):
                    return A_PT + sl * 1024 if sl < 2 else A_XN2 + (sl - 2) * 1024
                avlast = {0: 0, 1: 0, 2: 0, 3: 0}
                prev_f4 = [0, 0]
                prev_f5 = [0, 0]
                prev_f6 = [0, 0]
                c_f3_last = 0
                prev_exp = {0: 0, 1: 0}
                for qi in range(NTC):
                    nki = 4 * qi + 4
                    exp_done = {}
                    av_done = {}
                    msk_done = {}
                    for kit in range(nki + 2):
                        ki = kit
                        if ki < nki:
                            par = ki % 2
                            sl = ki % 4
                            def fsa(t, ki=ki, qi=qi, par=par):
                                return t.matmul(PSW[par][:, 0:512], ar(A_KR + ki * 128, 128),
                                                ar(A_QA + qi * TC, TC), start=True, stop=True,
                                                skip_group_check=True)
                            w_sa = []
                            if ki - 2 in exp_done:
                                w_sa.append(("scalar", exp_done[ki - 2]))
                            else:
                                w_sa.append(("scalar", prev_exp[par]))
                            c_sa = pl.emit("tensor", fsa, waits=w_sa, inc=1)
                            def fsb_(t, ki=ki, qi=qi, par=par):
                                return t.matmul(PSW[par][:, 512:1024], ar(A_KR + ki * 128, 128),
                                                ar(A_QB + qi * TC, TC), start=True, stop=True,
                                                skip_group_check=True)
                            c_sb = pl.emit("tensor", fsb_, waits=[], inc=1)
                            dj = ki - 4 * qi
                            def fex(s, par=par, sl=sl):
                                return s.activation(ar(pt_base(sl), 1024), PSW[par][:, 0:1024], AF.Exp,
                                                    bias=biasA[:], scale=1.0)
                            w_ex = [("tensor", c_sb)]
                            if ki - 4 in av_done:
                                w_ex.append(("tensor", av_done[ki - 4]))
                            else:
                                w_ex.append(("tensor", avlast[sl]))
                            c_ex = pl.emit("scalar", fex, waits=w_ex, inc=1)
                            exp_done[ki] = c_ex
                            prev_exp[par] = c_ex
                            if dj >= 0:
                                def fms(v, dj=dj, sl=sl):
                                    v.tensor_tensor(ar(pt_base(sl), TC), ar(pt_base(sl), TC),
                                                    ar(A_DMSK + dj * 512, TC), ALU.mult)
                                    return v.tensor_tensor(ar(pt_base(sl) + 512, TC), ar(pt_base(sl) + 512, TC),
                                                           ar(A_DMSK + dj * 512, TC), ALU.mult)
                                msk_done[ki] = pl.emit("vector", fms, waits=[("scalar", c_ex)], inc=1)
                        ka = kit - 2
                        if ka >= 0 and ka < nki:
                            sla = ka % 4
                            if ka in msk_done:
                                dep = [("vector", msk_done[ka])]
                            else:
                                dep = [("scalar", exp_done[ka])]
                            if ka == 0:
                                dep.append(("vector", prev_f5[0]))
                                dep.append(("vector", prev_f5[1]))
                            def fava(t, ka=ka, sla=sla, nki=nki):
                                return t.matmul(PS[4][0:65, :], ar(A_VT + ka * VTS, 65),
                                                ar(pt_base(sla), TC), start=(ka == 0), stop=(ka == nki - 1),
                                                skip_group_check=True)
                            pl.emit("tensor", fava, waits=dep, inc=1)
                            def favb(t, ka=ka, sla=sla, nki=nki):
                                return t.matmul(PS[5][0:65, :], ar(A_VT + ka * VTS + 66, 65),
                                                ar(pt_base(sla) + 512, TC), start=(ka == 0), stop=(ka == nki - 1),
                                                skip_group_check=True)
                            av_done[ka] = pl.emit("tensor", favb, waits=[], inc=1)
                    for sl in range(4):
                        kk = nki - 1 - ((nki - 1 - sl) % 4)
                        if kk in av_done:
                            avlast[sl] = av_done[kk]
                    for h in range(2):
                        def ffin1(s, h=h):
                            s.activation(PS[6 + h][64:65, :], PS[4 + h][64:65, :], AF.Ln, bias=0.0, scale=1.0)
                            return s.activation(PS[4 + h][64:65, :], PS[6 + h][64:65, :], AF.Exp,
                                                bias=0.0, scale=-1.0)
                        c_f1 = pl.emit("scalar", ffin1, waits=[("tensor", av_done[nki - 1]),
                                                               ("vector", prev_f4[h])], inc=1)
                        def ffin2(v, h=h):
                            return v.tensor_copy(ar(A_RROW, TC, (64, 65)), PS[4 + h][64:65, :])
                        c_f2 = pl.emit("vector", ffin2, waits=[("scalar", c_f1),
                                                               ("tensor", c_f3_last)], inc=1)
                        def ffin3(t, h=h):
                            return t.matmul(PS[6 + h][:, :], sb["ones2d"][64:65, 0:128],
                                            ar(A_RROW, TC, (64, 65)), start=True, stop=True)
                        c_f3 = pl.emit("tensor", ffin3, waits=[("vector", c_f2)], inc=1)
                        c_f3_last = c_f3
                        def ffin4(v, h=h):
                            return v.tensor_copy(ar(A_FRBC + h * 512, TC), PS[6 + h][:, :])
                        c_f4 = pl.emit("vector", ffin4, waits=[("tensor", c_f3)], inc=1)
                        prev_f4[h] = c_f4
                        def ffin5(v, h=h):
                            return v.tensor_mul(ar(A_FOUT + h * 512, TC, (0, 64)),
                                                PS[4 + h][0:64, :], ar(A_FRBC + h * 512, TC, (0, 64)))
                        c_f5 = pl.emit("vector", ffin5, waits=[("sync", prev_f6[h])], inc=1)
                        prev_f5[h] = c_f5
                        def ffin6(s_, h=h, qi=qi, hp=hp):
                            return s_.dma_start(out=attn_h[hp][64 * h:64 * h + 64, qi * TC:(qi + 1) * TC],
                                                in_=ar(A_FOUT + h * 512, TC, (0, 64)))
                        prev_f6[h] = pl.emit("sync", ffin6, waits=[("vector", c_f5)], inc=16)
                pl.barrier()
            if os.environ.get("KATTNCUT", "") == "flash":
                return
            def fwo(g):
                return g.dma_start(out=ar(A_WO, 8192).rearrange("p (kt c) -> p kt c", c=1024),
                                   in_=woT.rearrange("(kt p) c -> p kt c", p=128))
            c_wo = pl.emit("gpsimd", fwo, inc=16)
            ra_hist = {}
            och_hist = {}
            for tc in range(NTC):
                par = tc % 2
                def fas(s, tc=tc, par=par):
                    return s.dma_start(out=ar(A_AST + par * 4096, 4096)
                                       .rearrange("p (kt t) -> p kt t", t=TC),
                                       in_=attn_h.rearrange("hp p q -> p hp q")[:, :, tc * TC:(tc + 1) * TC])
                w_as = [("tensor", och_hist.get(tc - 2, 0))]
                c_as = pl.emit("sync", fas, waits=w_as, inc=16)
                for ht in range(NKT):
                    ob = PS[ht % 2]
                    def fo(t, ht=ht, par=par, ob=ob):
                        for kt in range(NKT):
                            mm = t.matmul(ob[:, :], ar(A_WO + kt * 1024 + ht * 128, 128),
                                          ar(A_AST + par * 4096 + kt * TC, TC),
                                          start=(kt == 0), stop=(kt == NKT - 1))
                        return mm
                    w_o = [("sync", c_as), ("gpsimd", c_wo)]
                    if (tc, ht - 2) in ra_hist:
                        w_o.append(("vector", ra_hist[(tc, ht - 2)]))
                    elif (tc - 1, 6 + ht) in ra_hist:
                        w_o.append(("vector", ra_hist[(tc - 1, 6 + ht)]))
                    c_o = pl.emit("tensor", fo, waits=w_o, inc=1)
                    och_hist[tc] = c_o
                    def fra(v, ht=ht, tc=tc, ob=ob):
                        return v.tensor_add(Rsl(ht, tc * TC, TC), Rsl(ht, tc * TC, TC), ob[:, :])
                    ra_hist[(tc, ht)] = pl.emit("vector", fra, waits=[("tensor", c_o)], inc=1)
            pl.barrier()

        def final_store():
            c = pl.emit("sync", lambda s: s.dma_start(
                out=outT.rearrange("(kt p) t -> p kt t", p=128),
                in_=R[:].rearrange("p (kt t) -> p kt t", t=T)), inc=16)
            pl.emit("sync", None, waits=[("sync", c)])

        init_phase()
        ln_phase(routing=True)
        expert_phase()
        if stage in ("attn", "full"):
            ln_phase(routing=False)
            if os.environ.get("KATTNCUT", "") != "ln2":
                attn_phase()
        if stage == "full":
            ln_phase(routing=True)
            expert_phase()
        final_store()

        def realize(eng_name):
            def runner(eng):
                for ws, fn, inc in pl.ops[eng_name]:
                    for we, wv in ws:
                        eng.wait_ge(sem[we], wv)
                    if fn is not None:
                        inst = fn(eng)
                        if inc:
                            inst.then_inc(sem[eng_name], inc)
            return runner

        @block.sync
        def _(sync):
            realize("sync")(sync)

        @block.gpsimd
        def _(gpsimd):
            realize("gpsimd")(gpsimd)

        @block.vector
        def _(vector):
            realize("vector")(vector)

        @block.scalar
        def _(scalar):
            realize("scalar")(scalar)

        @block.tensor
        def _(tensor):
            realize("tensor")(tensor)

    return nc


def kernel(L0_attn_x, L0_attn_wq, L0_attn_wk, L0_attn_wv, L0_attn_wo,
           L0_ffn_gate_w, L0_ffn_sh_gate, L0_ffn_sh_up, L0_ffn_sh_down,
           L0_ffn_rt_gate, L0_ffn_rt_up, L0_ffn_rt_down):
    stage = os.environ.get("KSTAGE", "full")
    debug = os.environ.get("KDEBUG", "") == "1"
    x = np.asarray(L0_attn_x, dtype=np.float32)
    f32c = lambda a: np.ascontiguousarray(np.asarray(a, dtype=np.float32))
    shared = {
        "wqT": f32c(np.asarray(L0_attn_wq).T), "wkT": f32c(np.asarray(L0_attn_wk).T),
        "wvT": f32c(np.asarray(L0_attn_wv).T), "woT": f32c(np.asarray(L0_attn_wo).T),
        "gw": f32c(L0_ffn_gate_w),
        "shg": f32c(L0_ffn_sh_gate), "shu": f32c(L0_ffn_sh_up), "shd": f32c(L0_ffn_sh_down),
        "rtg": f32c(L0_ffn_rt_gate), "rtu": f32c(L0_ffn_rt_up), "rtd": f32c(L0_ffn_rt_down),
    }
    shared.update(host_constants())
    nc = build_program(stage=stage, debug=debug)
    in_maps = []
    for c in range(8):
        m = dict(shared)
        m["xT"] = np.ascontiguousarray(x[c].T)
        in_maps.append(m)
    res = run_bass_kernel_spmd(nc, in_maps, core_ids=list(range(8)))
    if debug:
        kernel._dbg = res.results
    out = np.stack([np.ascontiguousarray(r["outT"].T) for r in res.results])
    return out
